# version 01
# baseline (speedup 1.0000x reference)
import contextlib
import math

import numpy as np

import concourse.bass as bass
import concourse.mybir as mybir
from concourse.bass_utils import run_bass_kernel_spmd

F32 = mybir.dt.float32
AF = mybir.ActivationFunctionType
ALU = mybir.AluOpType
AX = mybir.AxisListType

B, T, D = 4, 2048, 2048
H, KVH, DH = 16, 4, 128
TDH = 2 * DH
E, I_ = 8, 16
EPS = 1e-5
INV = 1.0 / math.sqrt(DH)
NCH = T // 128
HALF = T // 2
GN_N = float(T * TDH)


def _host_constants():
    f32 = np.float32
    ident = np.eye(128, dtype=f32)
    sq = np.arange(128)
    mask1 = (sq[:, None] <= sq[None, :]).astype(f32)
    mask2 = np.concatenate([mask1, mask1], axis=1)
    inv = (1.0 / (10000.0 ** (np.arange(0, DH, 2).astype(f32) / f32(DH)))).astype(f32)
    freqs = np.arange(T).astype(f32)[:, None] * inv[None, :]
    cos_t = np.cos(freqs).astype(f32)
    sin_t = np.sin(freqs).astype(f32)
    e8 = np.zeros((E, 128), dtype=f32)
    for e in range(E):
        e8[e, e * I_:(e + 1) * I_] = 1.0
    return ident, mask2, cos_t, sin_t, e8


def _prep_weights(inp):
    f = lambda a: np.ascontiguousarray(np.asarray(a, dtype=np.float32))
    w = {}
    w["wq"] = f(inp["L0_attn_Wq"])
    w["wk"] = f(inp["L0_attn_Wk"])
    w["wv"] = f(inp["L0_attn_Wv"])
    w["wo"] = f(inp["L0_attn_Wo"])
    w["gnw_fm"] = f(np.asarray(inp["L0_attn_gn_w"]).reshape(32, 128).T)
    w["gnb_fm"] = f(np.asarray(inp["L0_attn_gn_b"]).reshape(32, 128).T)
    w["lam_pack"] = f(np.concatenate([
        np.asarray(inp["L0_attn_lambda_q1"]), np.asarray(inp["L0_attn_lambda_k1"]),
        np.asarray(inp["L0_attn_lambda_q2"]), np.asarray(inp["L0_attn_lambda_k2"]),
    ]).reshape(1, 4 * DH))
    w["router_fm"] = f(np.asarray(inp["L0_ffn_router_w"]).T)
    w["down_all"] = f(np.asarray(inp["L0_ffn_down_w"]).transpose(2, 0, 1).reshape(D, 128))
    w["up_all"] = f(np.asarray(inp["L0_ffn_up_w"]).transpose(0, 2, 1).reshape(128, D))
    return w


class Sem:
    def __init__(self, ctx, nc, name):
        self.h = ctx.enter_context(nc.semaphore(name))
        self.n = 0

    def inc(self, inst, k=1):
        inst.then_inc(self.h, k)
        self.n += k
        return inst

    def wait(self, eng, n=None):
        eng.wait_ge(self.h, self.n if n is None else n)


STAGES = {"p1": 1, "p2": 2, "p3": 3, "p4": 4, "full": 5}


def build_nc(stage="full", nslabs=NCH):
    nc = bass.Bass("TRN2", target_bir_lowering=False)
    want = STAGES[stage]
    dbg = want < 5

    x_in = nc.dram_tensor("x", [T, D], F32, kind="ExternalInput").ap()
    wq_d = nc.dram_tensor("wq", [D, 4096], F32, kind="ExternalInput").ap()
    wk_d = nc.dram_tensor("wk", [D, 1024], F32, kind="ExternalInput").ap()
    wv_d = nc.dram_tensor("wv", [D, 1024], F32, kind="ExternalInput").ap()
    wo_d = nc.dram_tensor("wo", [4096, D], F32, kind="ExternalInput").ap()
    gnw_d = nc.dram_tensor("gnw_fm", [128, 32], F32, kind="ExternalInput").ap()
    gnb_d = nc.dram_tensor("gnb_fm", [128, 32], F32, kind="ExternalInput").ap()
    lam_d = nc.dram_tensor("lam_pack", [1, 4 * DH], F32, kind="ExternalInput").ap()
    rout_d = nc.dram_tensor("router_fm", [D, 8], F32, kind="ExternalInput").ap()
    down_d = nc.dram_tensor("down_all", [D, 128], F32, kind="ExternalInput").ap()
    up_d = nc.dram_tensor("up_all", [128, D], F32, kind="ExternalInput").ap()
    ident_d = nc.dram_tensor("ident", [128, 128], F32, kind="ExternalInput").ap()
    mask_d = nc.dram_tensor("mask2", [128, 256], F32, kind="ExternalInput").ap()
    cos_d = nc.dram_tensor("cos_t", [T, 64], F32, kind="ExternalInput").ap()
    sin_d = nc.dram_tensor("sin_t", [T, 64], F32, kind="ExternalInput").ap()
    e8_d = nc.dram_tensor("e8", [E, 128], F32, kind="ExternalInput").ap()
    hsel_d = nc.dram_tensor("halfsel", [128, 2], F32, kind="ExternalInput").ap()

    out_d = nc.dram_tensor("out", [HALF, D], F32, kind="ExternalOutput").ap()

    def scratch(name, shape, expose):
        kind = "ExternalOutput" if (dbg and expose) else "Internal"
        return nc.dram_tensor(name, shape, F32, kind=kind).ap()

    x1_d = scratch("x1_full", [T, D], want >= 1)
    xa2_d = scratch("xa2_fm", [D, T], want >= 1)
    qfm_d = scratch("q_fm", [4096, T], want == 2)
    kfm_d = scratch("k_fm", [1024, T], want == 2)
    v_d = scratch("v_sc", [KVH, T, TDH], want == 2)
    y_d = scratch("y_sc", [T, 4096], want == 3)
    if dbg:
        dbg_d = nc.dram_tensor("dbg_v", [128, 256], F32, kind="ExternalOutput").ap()
        x2_dbg = nc.dram_tensor("x2_dbg", [HALF, D], F32, kind="ExternalOutput").ap()
        dbg2_d = nc.dram_tensor("dbg2", [128, 3 * D + 512], F32, kind="ExternalOutput").ap()

    ctx = contextlib.ExitStack()
    with ctx:
        sb = lambda name, shape: ctx.enter_context(nc.sbuf_tensor(name, shape, F32))
        psum = lambda name: ctx.enter_context(nc.psum_tensor(name, [128, 512], F32))
        sem = lambda name: Sem(ctx, nc, name)
        block = ctx.enter_context(nc.Block())

        ident_s = sb("ident_s", [128, 128])
        mask_s = sb("mask_s", [128, 256])
        cos_s = sb("cos_s", [128, NCH * 64])
        sin_s = sb("sin_s", [128, NCH * 64])
        rout_s = sb("rout_s", [128, NCH * 8])
        down_s = sb("down_s", [128, NCH * 128])
        up_s = sb("up_s", [128, D])
        gnw_s = sb("gnw_s", [128, 32])
        gnb_s = sb("gnb_s", [128, 32])
        lam_s = sb("lam_s", [1, 4 * DH])
        e8_s = sb("e8_s", [E, 128])
        hsel_s = sb("hsel_s", [128, 2])
        ones_s = sb("ones_s", [128, 2])
        ones_row = sb("ones_row", [1, 128])
        lamneg_s = sb("lamneg_s", [128, 1])
        negb_s = sb("negb_s", [128, 1])
        qmax_s = sb("qmax_s", [128, 1])
        kmax_s = sb("kmax_s", [128, 1])
        stats_s = sb("stats_s", [128, 32])
        sctbl_s = sb("sctbl_s", [128, 32])
        bitbl_s = sb("bitbl_s", [128, 32])
        tiny = sb("tiny", [128, 256])
        tiny2 = sb("tiny2", [128, 32])

        pool = [sb(f"pool{i}", [128, D]) for i in range(12)]
        xs = pool[0:2]
        xa_tm = pool[2:4]
        xa_fm = pool[4:6]
        x1_tm = pool[6:8]
        xa2_tm = pool[8:10]
        xa2_fm = pool[10:12]

        a_sil = [sb("asil0", [128, 128]), sb("asil1", [128, 128])]
        ag_b = [sb("ag0", [128, 128]), sb("ag1", [128, 128])]
        gf_b = [sb("gf0", [8, 128]), sb("gf1", [8, 128])]
        sm_b = [sb("sm0", [128, 32]), sb("sm1", [128, 32])]
        w8_b = [sb("w80", [128, 8]), sb("w81", [128, 8])]
        g8_b = [sb("g80", [128, 8]), sb("g81", [128, 8])]
        t8_b = [sb("t80", [128, 8]), sb("t81", [128, 8])]

        vstage = [sb("vst0", [128, NCH * 257]), sb("vst1", [128, NCH * 257])]
        qstage = pool[8:10]
        qrope = pool[10:12]
        tstage = [sb("tst0", [128, 512]), sb("tst1", [128, 512])]
        sqb = [sb("sqb0", [128, 512]), sb("sqb1", [128, 512])]
        wslice = [sb(f"wsl{i}", [128, 512]) for i in range(4)]
        e12 = [sb(f"e12{i}", [128, 256]) for i in range(4)]
        ybuf = [sb("ybuf0", [128, 256]), sb("ybuf1", [128, 256])]
        odbuf = sb("odbuf", [128, 256])
        sqout = sb("sqout", [128, 256])
        bc32 = sb("bc32", [128, 32])
        yq = sb("yq", [128, 4 * 1024])

        pT0, pT1 = psum("pT0"), psum("pT1")
        pA, pB = psum("pA"), psum("pB")
        pU = [psum("pU0"), psum("pU1"), psum("pU2"), psum("pU3")]

        s_const = sem("s_const")
        s_x = sem("s_x")
        s_pe = sem("s_pe")
        s_sc = sem("s_sc")
        s_v = sem("s_v")
        s_g = sem("s_g")
        s_gd = sem("s_gd")
        s_w = sem("s_w")
        s_evac = sem("s_evac")
        s_exp = sem("s_exp")
        s_msk = sem("s_msk")
        s_nrm = sem("s_nrm")
        s_out = sem("s_out")

        def dchunk(buf, k):
            return buf[:, 128 * k:128 * (k + 1)]

        def ln_sum_v(v, src, smb):
            return v.reduce_sum(smb[:, 20:21], src, axis=AX.X)

        def ln_sq_sc(s, src, scratch, smb):
            return s.activation(scratch, src, AF.Square, accum_out=smb[:, 21:22])

        def ln_fin_v(v, smb):
            v.drain()
            v.tensor_scalar(smb[:, 24:25], smb[:, 20:21], 1.0 / D, None, ALU.mult)
            v.tensor_scalar(smb[:, 22:23], smb[:, 21:22], 1.0 / D, None, ALU.mult)
            v.drain()
            v.tensor_tensor(smb[:, 23:24], smb[:, 24:25], smb[:, 24:25], ALU.mult)
            v.drain()
            v.tensor_tensor(smb[:, 25:26], smb[:, 22:23], smb[:, 23:24], ALU.subtract)
            v.drain()
            return v.tensor_scalar(smb[:, 25:26], smb[:, 25:26], EPS, None, ALU.add)

        def rstd_sc(s, smb):
            s.activation(smb[:, 26:27], smb[:, 25:26], AF.Ln)
            return s.activation(smb[:, 26:27], smb[:, 26:27], AF.Exp, scale=-0.5)

        def emit_ffn(p, xa_fm_ap, res_ap, dst_ap, pre_waits):
            def pe_ffn(t, p=p, pw=tuple(pre_waits), nv=s_v.n, nsc=s_sc.n):
                for sh, n in pw:
                    t.wait_ge(sh, n)
                t.wait_ge(s_v.h, nv)
                t.wait_ge(s_sc.h, nsc)
                for k in range(NCH):
                    inst = t.matmul(pB[:, 0:8], xa_fm_ap[:, 128 * k:128 * (k + 1)],
                                    rout_s[:, 8 * k:8 * (k + 1)],
                                    start=(k == 0), stop=(k == NCH - 1))
                s_pe.inc(inst)
                for k in range(NCH):
                    inst = t.matmul(pA[:, 0:128], dchunk(down_s, k),
                                    xa_fm_ap[:, 128 * k:128 * (k + 1)],
                                    start=(k == 0), stop=(k == NCH - 1))
                s_pe.inc(inst)
            block.tensor(pe_ffn)

            def v_negm(v, p=p, n=s_pe.n - 1):
                v.wait_ge(s_pe.h, n)
                v.reduce_max(sm_b[p][:, 27:28], pB[:, 0:8], axis=AX.X)
                v.drain()
                s_v.inc(v.tensor_scalar(sm_b[p][:, 27:28], sm_b[p][:, 27:28], -1.0,
                                        None, ALU.mult))
            block.vector(v_negm)

            def sc_ffn(s, p=p, nv=s_v.n, npe=s_pe.n):
                s.wait_ge(s_v.h, nv)
                s.wait_ge(s_pe.h, npe)
                s.activation(w8_b[p][:], pB[:, 0:8], AF.Exp,
                             bias=sm_b[p][:, 27:28], accum_out=sm_b[p][:, 28:29])
                s_sc.inc(s.activation(a_sil[p][:], pA[:, 0:128], AF.Exp, scale=-1.0))
            block.scalar(sc_ffn)

            def v_ffn(v, p=p, n=s_sc.n):
                v.wait_ge(s_sc.h, n)
                v.tensor_scalar(a_sil[p][:], a_sil[p][:], 1.0, None, ALU.add)
                v.reciprocal(a_sil[p][:], a_sil[p][:])
                v.tensor_tensor(a_sil[p][:], pA[:, 0:128], a_sil[p][:], ALU.mult)
                v.reciprocal(sm_b[p][:, 29:30], sm_b[p][:, 28:29])
                v.reduce_max(sm_b[p][:, 30:31], w8_b[p][:], axis=AX.X)
                v.drain()
                v.tensor_tensor(g8_b[p][:], w8_b[p][:],
                                sm_b[p][:, 30:31].broadcast_to([128, 8]), ALU.is_ge)
                v.drain()
                v.scalar_tensor_tensor(t8_b[p][:], g8_b[p][:], -1e30, w8_b[p][:],
                                       ALU.mult, ALU.add)
                v.drain()
                v.reduce_max(sm_b[p][:, 31:32], t8_b[p][:], axis=AX.X)
                v.drain()
                v.tensor_tensor(g8_b[p][:], w8_b[p][:],
                                sm_b[p][:, 31:32].broadcast_to([128, 8]), ALU.is_ge)
                v.drain()
                s_v.inc(v.scalar_tensor_tensor(g8_b[p][:], g8_b[p][:], sm_b[p][:, 29:30],
                                               w8_b[p][:], ALU.mult, ALU.mult))
            block.vector(v_ffn)

            def pe_gate2(t, p=p, n=s_v.n):
                t.wait_ge(s_v.h, n)
                s_pe.inc(t.transpose(pB[0:8, 128:256], g8_b[p][:], ident_s[:]))
            block.tensor(pe_gate2)

            def sc_gate2(s, p=p, n=s_pe.n):
                s.wait_ge(s_pe.h, n)
                s_sc.inc(s.copy(gf_b[p][:], pB[0:8, 128:256]))
            block.scalar(sc_gate2)

            def pe_expand2(t, p=p, n=s_sc.n):
                t.wait_ge(s_sc.h, n)
                s_pe.inc(t.matmul(pB[:, 256:384], e8_s[:], gf_b[p][:],
                                  start=True, stop=True))
            block.tensor(pe_expand2)

            def v_ag2(v, p=p, n=s_pe.n):
                v.wait_ge(s_pe.h, n)
                s_v.inc(v.tensor_tensor(ag_b[p][:], a_sil[p][:], pB[:, 256:384], ALU.mult))
            block.vector(v_ag2)

            def pe_up2(t, p=p, n=s_v.n):
                t.wait_ge(s_v.h, n)
                for dq in range(4):
                    s_pe.inc(t.matmul(pU[dq][:], ag_b[p][:],
                                      up_s[:, 512 * dq:512 * (dq + 1)],
                                      start=True, stop=True))
            block.tensor(pe_up2)

            def v_out2(v, p=p, n=s_pe.n):
                v.wait_ge(s_pe.h, n)
                for dq in range(4):
                    inst = v.scalar_tensor_tensor(dst_ap[:, 512 * dq:512 * (dq + 1)],
                                                  pU[dq][:], 0.5,
                                                  res_ap[:, 512 * dq:512 * (dq + 1)],
                                                  ALU.mult, ALU.add)
                s_v.inc(inst)
            block.vector(v_out2)

        evac_round = [0]

        def transposes16(src_tm, dst_fm, pre_wait=None):
            for r in range(4):
                rg = evac_round[0]
                bank = (pT0, pT1)[rg % 2]

                def pe_round(t, bank=bank, r=r, rg=rg, src_tm=src_tm, pre_wait=pre_wait):
                    if r == 0 and pre_wait is not None:
                        pre_wait(t)
                    if rg >= 2:
                        s_evac.wait(t, rg - 1)
                    for i in range(4):
                        inst = t.transpose(bank[:, 128 * i:128 * (i + 1)],
                                           dchunk(src_tm, 4 * r + i), ident_s[:])
                    s_pe.inc(inst)
                block.tensor(pe_round)

                def sc_round(s, bank=bank, r=r, dst_fm=dst_fm, n=s_pe.n):
                    s.wait_ge(s_pe.h, n)
                    s_evac.inc(s.copy(dst_fm[:, 512 * r:512 * (r + 1)], bank[:]))
                block.scalar(sc_round)
                evac_round[0] += 1

        def p0_sync(sy):
            s_x.inc(sy.dma_start(out=xs[0][:], in_=x_in[0:128, :]), 16)
            s_x.inc(sy.dma_start(out=xs[1][:], in_=x_in[128:256, :]), 16)
            for dst, src_ in [(ident_s[:], ident_d), (mask_s[:], mask_d),
                              (gnw_s[:], gnw_d), (gnb_s[:], gnb_d),
                              (lam_s[:], lam_d), (e8_s[:], e8_d), (up_s[:], up_d),
                              (hsel_s[:], hsel_d)]:
                s_const.inc(sy.dma_start(out=dst, in_=src_), 16)
            for dst, src_ in [(cos_s, cos_d), (sin_s, sin_d),
                              (rout_s, rout_d), (down_s, down_d)]:
                s_const.inc(sy.dma_start(
                    out=dst[:].rearrange("p (b c) -> p b c", b=NCH),
                    in_=src_.rearrange("(b p) c -> p b c", p=128)), 16)
        block.sync(p0_sync)

        def p0_vec(v):
            v.memset(ones_s[:], 0.0)
            v.memset(ones_s[:, 0:1], 1.0)
            v.memset(ones_row[:], 1.0)
            v.memset(qmax_s[:], 0.0)
            v.memset(kmax_s[:], 0.0)
            v.memset(stats_s[:], 0.0)
            for vs in vstage:
                v.memset(vs[:].rearrange("p (b c) -> p b c", c=257)[:, :, 256:257], 1.0)
            s_const.wait(v)
            v.tensor_tensor(tiny[0:1, 2:2 + DH], lam_s[:, 0:DH], lam_s[:, DH:2 * DH], ALU.mult)
            v.drain()
            v.reduce_sum(tiny[0:1, 0:1], tiny[0:1, 2:2 + DH], axis=AX.X)
            v.drain()
            v.tensor_tensor(tiny[0:1, 2:2 + DH], lam_s[:, 2 * DH:3 * DH],
                            lam_s[:, 3 * DH:4 * DH], ALU.mult)
            v.drain()
            s_v.inc(v.reduce_sum(tiny[0:1, 1:2], tiny[0:1, 2:2 + DH], axis=AX.X))
        block.vector(p0_vec)

        def p0_sc(s):
            s.add_instruction(mybir.InstLoadActFuncSet(
                name=nc.get_next_instruction_name(), act_func_set_id=6, ins=[], outs=[]))
            s_v.wait(s)
            s_sc.inc(s.activation(tiny[0:1, 4:6], tiny[0:1, 0:2], AF.Exp))
        block.scalar(p0_sc)

        def p0_vec2(v):
            s_sc.wait(v)
            v.tensor_tensor(tiny[0:1, 6:7], tiny[0:1, 4:5], tiny[0:1, 5:6], ALU.subtract)
            v.drain()
            s_v.inc(v.tensor_scalar(tiny[0:1, 6:7], tiny[0:1, 6:7], 0.2, -1.0,
                                    ALU.add, ALU.mult))
        block.vector(p0_vec2)

        def p0_pe_bcast(t):
            s_v.wait(t)
            s_pe.inc(t.matmul(pA[:, 508:509], ones_row[:], tiny[0:1, 6:7],
                              start=True, stop=True))
        block.tensor(p0_pe_bcast)

        def p0_v_bcast(v, n=s_pe.n):
            v.wait_ge(s_pe.h, n)
            s_v.inc(v.tensor_copy(lamneg_s[:], pA[:, 508:509]))
        block.vector(p0_v_bcast)

        for sl in range(nslabs):
            p = sl % 2

            def v_ln1(v, p=p, sl=sl):
                s_x.wait(v, 16 * (sl + 1))
                s_v.inc(ln_sum_v(v, xs[p][:], sm_b[p]))
            block.vector(v_ln1)

            def sc_sq1(s, p=p, sl=sl):
                s_x.wait(s, 16 * (sl + 1))
                s_sc.inc(ln_sq_sc(s, xs[p][:], xa_tm[p][:], sm_b[p]))
            block.scalar(sc_sq1)

            def v_fin1(v, p=p, n=s_sc.n):
                v.wait_ge(s_sc.h, n)
                s_v.inc(ln_fin_v(v, sm_b[p]))
            block.vector(v_fin1)

            def sc_ln1(s, p=p, n=s_v.n):
                s.wait_ge(s_v.h, n)
                s_sc.inc(rstd_sc(s, sm_b[p]))
            block.scalar(sc_ln1)

            def v_xa1(v, p=p, n=s_sc.n):
                v.wait_ge(s_sc.h, n)
                s_v.inc(v.tensor_scalar(xa_tm[p][:], xs[p][:], sm_b[p][:, 24:25],
                                        sm_b[p][:, 26:27], ALU.subtract, ALU.mult))
            block.vector(v_xa1)

            def pe_pre(t, n=s_v.n):
                s_const.wait(t)
                t.wait_ge(s_v.h, n)
            transposes16(xa_tm[p], xa_fm[p], pre_wait=pe_pre)

            emit_ffn(p, xa_fm[p][:], xs[p][:], x1_tm[p][:],
                     [(s_evac.h, s_evac.n), (s_gd.h, s_gd.n)])

            def v_ln2sum(v, p=p, n=s_v.n):
                v.wait_ge(s_v.h, n)
                s_v.inc(ln_sum_v(v, x1_tm[p][:], sm_b[p]))
            block.vector(v_ln2sum)

            def sc_sq2(s, p=p, n=s_v.n):
                s.wait_ge(s_v.h, n)
                s_sc.inc(ln_sq_sc(s, x1_tm[p][:], xa2_tm[p][:], sm_b[p]))
            block.scalar(sc_sq2)

            def v_fin2(v, p=p, n=s_sc.n):
                v.wait_ge(s_sc.h, n)
                s_v.inc(ln_fin_v(v, sm_b[p]))
            block.vector(v_fin2)
            x_free_n = s_v.n - 1

            def g_x1(g, p=p, sl=sl, n=s_v.n - 1):
                g.wait_ge(s_v.h, n)
                s_gd.inc(g.dma_start(out=x1_d[128 * sl:128 * (sl + 1), :],
                                     in_=x1_tm[p][:]), 16)
            block.gpsimd(g_x1)

            def sc_ln2(s, p=p, n=s_v.n):
                s.wait_ge(s_v.h, n)
                s_sc.inc(rstd_sc(s, sm_b[p]))
            block.scalar(sc_ln2)

            def v_xa2(v, p=p, n=s_sc.n):
                v.wait_ge(s_sc.h, n)
                s_v.inc(v.tensor_scalar(xa2_tm[p][:], x1_tm[p][:], sm_b[p][:, 24:25],
                                        sm_b[p][:, 26:27], ALU.subtract, ALU.mult))
            block.vector(v_xa2)

            def pe_pre2(t, n=s_v.n, ngd=s_gd.n):
                t.wait_ge(s_v.h, n)
            transposes16(xa2_tm[p], xa2_fm[p], pre_wait=pe_pre2)

            def g_xa2(g, p=p, sl=sl, n=s_evac.n):
                g.wait_ge(s_evac.h, n)
                s_gd.inc(g.dma_start(
                    out=xa2_d[:, 128 * sl:128 * (sl + 1)].rearrange("(k p) t -> p k t", p=128),
                    in_=xa2_fm[p][:].rearrange("p (k t) -> p k t", k=NCH)), 16)
            block.gpsimd(g_xa2)

            if dbg and sl == 0:
                def v_dbg(v, nsc=s_sc.n):
                    v.wait_ge(s_sc.h, nsc)
                    v.tensor_copy(tiny[:, 100:108], pB[:, 0:8])
                    v.tensor_copy(tiny[:, 108:116], rout_s[:, 0:8])
                    s_v.inc(v.tensor_copy(tiny[:, 116:124], rout_s[:, 120:128]))
                block.vector(v_dbg)

                def g_dbg(g, p=p, nv=s_v.n, nsc=s_sc.n):
                    g.wait_ge(s_v.h, nv)
                    g.wait_ge(s_sc.h, nsc)
                    s_gd.inc(g.dma_start(out=dbg2_d[:, 0:D], in_=xa_tm[p][:]), 16)
                    s_gd.inc(g.dma_start(out=dbg2_d[:, D:2 * D], in_=xa_fm[p][:]), 16)
                    s_gd.inc(g.dma_start(out=dbg2_d[:, 2 * D:3 * D], in_=x1_tm[p][:]), 16)
                    s_gd.inc(g.dma_start(out=dbg2_d[:, 3 * D:3 * D + 128], in_=a_sil[p][:]), 16)
                    s_gd.inc(g.dma_start(out=dbg2_d[:, 3 * D + 128:3 * D + 256], in_=ag_b[p][:]), 16)
                    s_gd.inc(g.dma_start(out=dbg2_d[:, 3 * D + 256:3 * D + 264], in_=w8_b[p][:]), 16)
                    s_gd.inc(g.dma_start(out=dbg2_d[:, 3 * D + 264:3 * D + 272], in_=g8_b[p][:]), 16)
                    s_gd.inc(g.dma_start(out=dbg2_d[:, 3 * D + 272:3 * D + 304], in_=sm_b[p][:]), 16)
                block.gpsimd(g_dbg)

            if sl + 2 < NCH:
                def sy_pref(sy, sl=sl, n=x_free_n):
                    sy.wait_ge(s_v.h, n)
                    s_x.inc(sy.dma_start(out=xs[sl % 2][:],
                                         in_=x_in[128 * (sl + 2):128 * (sl + 3), :]), 16)
                block.sync(sy_pref)

        p1_gd_n = s_gd.n

        if want >= 2:
            p1_v_end, p1_pe_end, p1_sc_end = s_v.n, s_pe.n, s_sc.n
            wavebuf = pool[0:8]
            NW = 4
            wsl_ld = []
            wsl_use = []
            evac_prev = {}
            rope_prev = {}
            tst_store = {}
            gseq = [0]

            def wsrc(g):
                if g < 8:
                    return wq_d, 512 * g
                if g < 10:
                    return wk_d, 512 * (g - 8)
                return wv_d, 512 * (g - 10)

            for w in range(NW):
                def sy_wave(sy, w=w, npe=s_pe.n):
                    if w < 2:
                        sy.wait_ge(s_v.h, p1_v_end)
                        sy.wait_ge(s_pe.h, p1_pe_end)
                    else:
                        sy.wait_ge(s_pe.h, npe)
                    for c in range(4):
                        s_x.inc(sy.dma_start(
                            out=wavebuf[4 * (w % 2) + c][:].rearrange(
                                "p (k t) -> p k t", k=NCH),
                            in_=xa2_d[:, 128 * (4 * w + c):128 * (4 * w + c) + 128]
                            .rearrange("(k p) t -> p k t", p=128)), 16)
                block.sync(sy_wave)
                xa2_ld_n = s_x.n

                for g in range(12):
                    gi = gseq[0]
                    wd, col0 = wsrc(g)

                    base = len(wsl_ld)

                    def sy_prime(sy, wd=wd, col0=col0, base=base):
                        for d in range(4):
                            si = base + d
                            if si >= 4:
                                sy.wait_ge(s_pe.h, wsl_use[si - 4])
                            s_w.inc(sy.dma_start(
                                out=wslice[si % 4][:],
                                in_=wd[128 * d:128 * (d + 1), col0:col0 + 512]), 16)
                            wsl_ld.append(s_w.n)
                    block.sync(sy_prime)

                    def pe_start(t, gi=gi, nx=xa2_ld_n):
                        t.wait_ge(s_x.h, nx)
                        if gi == 0:
                            t.wait_ge(s_v.h, p1_v_end)
                        else:
                            t.wait_ge(s_sc.h, evac_prev[gi - 1])
                    block.tensor(pe_start)

                    for d in range(16):
                        def pe_d(t, w=w, d=d, base=base):
                            t.wait_ge(s_w.h, wsl_ld[base + d])
                            for c in range(4):
                                inst = t.matmul(pU[c][:],
                                                dchunk(wavebuf[4 * (w % 2) + c], d),
                                                wslice[(base + d) % 4][:],
                                                start=(d == 0), stop=(d == 15))
                            s_pe.inc(inst)
                            wsl_use.append(s_pe.n)
                        block.tensor(pe_d)
                        if d + 4 < 16:
                            def sy_next(sy, wd=wd, col0=col0, d=d, base=base):
                                si = base + d + 4
                                sy.wait_ge(s_pe.h, wsl_use[si - 4])
                                s_w.inc(sy.dma_start(
                                    out=wslice[si % 4][:],
                                    in_=wd[128 * (d + 4):128 * (d + 5),
                                           col0:col0 + 512]), 16)
                                wsl_ld.append(s_w.n)
                            block.sync(sy_next)
                    mm_n = s_pe.n

                    def sc_ev(s, g=g, gi=gi, mm_n=mm_n, ngd=s_gd.n):
                        s.wait_ge(s_pe.h, mm_n)
                        if gi >= 2:
                            s.wait_ge(s_v.h, rope_prev.get(gi - 2, 0))
                            s.wait_ge(s_gd.h, ngd)
                        for c in range(4):
                            inst = s.copy(qstage[g % 2][:, 512 * c:512 * (c + 1)], pU[c][:])
                        s_sc.inc(inst)
                    block.scalar(sc_ev)
                    evac_prev[gi] = s_sc.n
                    ev_n = s_sc.n

                    if g < 10:
                        def v_rope_c(v, w=w, g=g, ev_n=ev_n):
                            v.wait_ge(s_sc.h, ev_n)
                            for c in range(4):
                                qv = qstage[g % 2][:, 512 * c:512 * (c + 1)].rearrange(
                                    "p (m pair two) -> p m pair two", m=4, two=2)
                                rv = qrope[g % 2][:, 512 * c:512 * (c + 1)].rearrange(
                                    "p (m pair two) -> p m pair two", m=4, two=2)
                                cw = cos_s[:, 64 * (4 * w + c):64 * (4 * w + c) + 64]
                                cw = cw.unsqueeze(1).broadcast_to([128, 4, 64])
                                sw = sin_s[:, 64 * (4 * w + c):64 * (4 * w + c) + 64]
                                sw = sw.unsqueeze(1).broadcast_to([128, 4, 64])
                                qe = qv[:, :, :, 0]
                                qo = qv[:, :, :, 1]
                                re_ = rv[:, :, :, 0]
                                ro = rv[:, :, :, 1]
                                tmp = sqb[1][:, 0:256].rearrange("p (m pair) -> p m pair", m=4)
                                v.tensor_tensor(re_, qe, cw, ALU.mult)
                                v.tensor_tensor(tmp, qo, sw, ALU.mult)
                                v.tensor_tensor(re_, re_, tmp, ALU.subtract)
                                v.tensor_tensor(ro, qo, cw, ALU.mult)
                                v.tensor_tensor(tmp, qe, sw, ALU.mult)
                                inst = v.tensor_tensor(ro, ro, tmp, ALU.add)
                            s_v.inc(inst)
                        block.vector(v_rope_c)
                        rope_n = s_v.n

                        acc = qmax_s if g < 8 else kmax_s
                        for c in range(4):
                            def sc_sqc(s, g=g, c=c, ev_n=ev_n, nv=s_v.n):
                                s.wait_ge(s_sc.h, ev_n)
                                s.wait_ge(s_v.h, nv)
                                s_sc.inc(s.activation(
                                    sqb[0][:], qstage[g % 2][:, 512 * c:512 * (c + 1)],
                                    AF.Square))
                            block.scalar(sc_sqc)

                            def v_nrm(v, acc=acc, n=s_sc.n):
                                v.wait_ge(s_sc.h, n)
                                v.tensor_reduce(tiny2[:, 8:12].unsqueeze(2),
                                                sqb[0][:].rearrange("p (m f) -> p m f", m=4),
                                                axis=AX.X, op=ALU.add)
                                v.drain()
                                v.reduce_max(tiny2[:, 12:13], tiny2[:, 8:12], axis=AX.X)
                                v.drain()
                                s_v.inc(v.tensor_tensor(acc[:], acc[:], tiny2[:, 12:13],
                                                        ALU.max))
                            block.vector(v_nrm)
                        rope_prev[gi] = s_v.n

                        for c in range(4):
                            rg = evac_round[0]
                            bank = (pT0, pT1)[rg % 2]

                            def pe_tp(t, bank=bank, g=g, c=c, rg=rg, rope_n=rope_n):
                                t.wait_ge(s_v.h, rope_n)
                                if rg >= 2:
                                    s_evac.wait(t, rg - 1)
                                for i in range(4):
                                    inst = t.transpose(
                                        bank[:, 128 * i:128 * (i + 1)],
                                        qrope[g % 2][:, 512 * c + 128 * i:
                                                      512 * c + 128 * (i + 1)],
                                        ident_s[:])
                                s_pe.inc(inst)
                            block.tensor(pe_tp)

                            def sc_tev(s, bank=bank, rg=rg, n=s_pe.n,
                                       ngd=tst_store.get(evac_round[0] - 2)):
                                s.wait_ge(s_pe.h, n)
                                if ngd is not None:
                                    s.wait_ge(s_gd.h, ngd)
                                s_evac.inc(s.copy(tstage[rg % 2][:], bank[:]))
                            block.scalar(sc_tev)

                            def g_st(gp, w=w, g=g, c=c, rg=rg, n=s_evac.n):
                                gp.wait_ge(s_evac.h, n)
                                dst = qfm_d if g < 8 else kfm_d
                                r0 = 512 * g if g < 8 else 512 * (g - 8)
                                t0 = 128 * (4 * w + c)
                                s_gd.inc(gp.dma_start(
                                    out=dst[r0:r0 + 512, t0:t0 + 128].rearrange(
                                        "(i p) t -> p i t", p=128),
                                    in_=tstage[rg % 2][:].rearrange(
                                        "p (i t) -> p i t", i=4)), 16)
                            block.gpsimd(g_st)
                            tst_store[evac_round[0]] = s_gd.n
                            evac_round[0] += 1
                    else:
                        gv = g - 10
                        rope_prev[gi] = s_v.n
                        for c in range(4):
                            def g_stv(gp, w=w, c=c, gv=gv, g=g, n=ev_n):
                                gp.wait_ge(s_sc.h, n)
                                r0 = 128 * (4 * w + c)
                                s_gd.inc(gp.dma_start(
                                    out=v_d[2 * gv, r0:r0 + 128, :],
                                    in_=qstage[g % 2][:, 512 * c:512 * c + 256]), 16)
                                s_gd.inc(gp.dma_start(
                                    out=v_d[2 * gv + 1, r0:r0 + 128, :],
                                    in_=qstage[g % 2][:, 512 * c + 256:512 * c + 512]), 16)
                            block.gpsimd(g_stv)
                    gseq[0] += 1

            def pe_negb(t, nv=s_v.n, ne=s_evac.n):
                t.wait_ge(s_v.h, nv)
                s_evac.wait(t, ne)
                t.transpose(pT0[0:1, 0:128], qmax_s[:], ident_s[:])
                s_pe.inc(t.transpose(pT0[0:1, 128:256], kmax_s[:], ident_s[:]))
            block.tensor(pe_negb)

            def sc_negb(s, n=s_pe.n):
                s.wait_ge(s_pe.h, n)
                s_sc.inc(s.copy(tiny[0:1, 0:256], pT0[0:1, 0:256]))
            block.scalar(sc_negb)

            def v_negb(v, n=s_sc.n):
                v.wait_ge(s_sc.h, n)
                v.reduce_max(tiny2[0:1, 0:1], tiny[0:1, 0:128], axis=AX.X)
                v.reduce_max(tiny2[0:1, 1:2], tiny[0:1, 128:256], axis=AX.X)
                v.drain()
                s_v.inc(v.tensor_tensor(tiny2[0:1, 2:3], tiny2[0:1, 0:1],
                                        tiny2[0:1, 1:2], ALU.mult))
            block.vector(v_negb)

            def sc_negb2(s, n=s_v.n):
                s.wait_ge(s_v.h, n)
                s.activation(tiny2[0:1, 3:4], tiny2[0:1, 2:3], AF.Ln)
                s_sc.inc(s.activation(tiny2[0:1, 3:4], tiny2[0:1, 3:4], AF.Exp, scale=0.5))
            block.scalar(sc_negb2)

            def v_negb2(v, n=s_sc.n):
                v.wait_ge(s_sc.h, n)
                s_v.inc(v.tensor_scalar(tiny2[0:1, 4:5], tiny2[0:1, 3:4], -INV,
                                        None, ALU.mult))
            block.vector(v_negb2)

            def pe_negb2(t, n=s_v.n):
                t.wait_ge(s_v.h, n)
                s_pe.inc(t.matmul(pT1[:, 0:1], ones_row[:], tiny2[0:1, 4:5],
                                  start=True, stop=True))
            block.tensor(pe_negb2)

            def v_negb3(v, n=s_pe.n):
                v.wait_ge(s_pe.h, n)
                s_v.inc(v.tensor_copy(negb_s[:], pT1[:, 0:1]))
            block.vector(v_negb3)

        if want >= 3:
            p2_pe_end, p2_v_end, p2_sc_end = s_pe.n, s_v.n, s_sc.n
            negb_n = s_v.n
            kbuf = [[pool[0], pool[1]], [pool[2], pool[3]]]
            qbuf = [[pool[4], pool[5]], [pool[6], pool[7]]]
            kv_ld = {}
            q_ld = {}
            pe_g_end = {}
            pe_h_end = {}
            exp_cnt = {}
            av_cnt = {}
            msk_cnt = {}
            nrm_cnt = {}
            sq_cnt = {}
            yst_cnt = {}
            jg_ctr = [0]
            qbg_ctr = [0]

            def emit_kv_load(g):
                def sy_kv(sy, g=g, nprev=pe_g_end.get(g - 2)):
                    if g < 2:
                        sy.wait_ge(s_pe.h, p2_pe_end)
                        sy.wait_ge(s_v.h, p2_v_end)
                    else:
                        sy.wait_ge(s_pe.h, nprev)
                    s_x.inc(sy.dma_start(out=kbuf[g % 2][0][:],
                                         in_=kfm_d[256 * g:256 * g + 128, :]), 16)
                    s_x.inc(sy.dma_start(out=kbuf[g % 2][1][:],
                                         in_=kfm_d[256 * g + 128:256 * g + 256, :]), 16)
                    s_x.inc(sy.dma_start(
                        out=vstage[g % 2][:].rearrange("p (b c) -> p b c", c=257)[:, :, 0:256],
                        in_=v_d[g].rearrange("(b p) c -> p b c", p=128)), 16)
                block.sync(sy_kv)
                kv_ld[g] = s_x.n

            def emit_q_load(h):
                def sy_q(sy, h=h, nprev=pe_h_end.get(h - 2)):
                    if h < 2:
                        sy.wait_ge(s_pe.h, p2_pe_end)
                        sy.wait_ge(s_v.h, p2_v_end)
                    else:
                        sy.wait_ge(s_pe.h, nprev)
                    s_x.inc(sy.dma_start(out=qbuf[h % 2][0][:],
                                         in_=qfm_d[256 * h:256 * h + 128, :]), 16)
                    s_x.inc(sy.dma_start(out=qbuf[h % 2][1][:],
                                         in_=qfm_d[256 * h + 128:256 * h + 256, :]), 16)
                block.sync(sy_q)
                q_ld[h] = s_x.n

            emit_kv_load(0)
            emit_q_load(0)

            for g in range(KVH):
                for hh in range(4):
                    h = 4 * g + hh
                    if h + 1 < H:
                        emit_q_load(h + 1)
                    if hh == 1 and g + 1 < KVH:
                        emit_kv_load(g + 1)

                    for qb in range(NCH):
                        qbg = qbg_ctr[0]
                        par = qbg % 2
                        o1, o2 = pU[2 * par], pU[2 * par + 1]
                        jg0 = jg_ctr[0]

                        def emit_av(j, jg):
                            def pe_av(t, g=g, j=j, jg=jg, qb=qb, qbg=qbg, par=par,
                                      o1=o1, o2=o2):
                                t.wait_ge(s_exp.h, exp_cnt[jg])
                                if jg in msk_cnt:
                                    t.wait_ge(s_msk.h, msk_cnt[jg])
                                if j == 0 and qbg >= 2:
                                    t.wait_ge(s_nrm.h, nrm_cnt[qbg - 2])
                                vb = vstage[g % 2][:, 257 * j:257 * (j + 1)]
                                t.matmul(o1[:, 0:257], e12[jg % 4][:, 0:128], vb,
                                         start=(j == 0), stop=(j == qb))
                                s_pe.inc(t.matmul(o2[:, 0:257], e12[jg % 4][:, 128:256], vb,
                                                  start=(j == 0), stop=(j == qb)))
                            block.tensor(pe_av)
                            av_cnt[jg] = s_pe.n

                        for j in range(qb + 1):
                            jg = jg_ctr[0]
                            sbank = (pA, pB, pT0, pT1)[jg % 4]

                            def pe_score(t, g=g, h=h, qb=qb, j=j, jg=jg,
                                         sbank=sbank):
                                if jg >= 4:
                                    t.wait_ge(s_exp.h, exp_cnt[jg - 4])
                                t.wait_ge(s_x.h, max(kv_ld[g], q_ld[h]))
                                t.matmul(sbank[:, 0:128], kbuf[g % 2][0][:, 128 * j:128 * (j + 1)],
                                         qbuf[h % 2][0][:, 128 * qb:128 * (qb + 1)],
                                         start=True, stop=True)
                                s_pe.inc(t.matmul(sbank[:, 128:256],
                                                  kbuf[g % 2][1][:, 128 * j:128 * (j + 1)],
                                                  qbuf[h % 2][1][:, 128 * qb:128 * (qb + 1)],
                                                  start=True, stop=True))
                            block.tensor(pe_score)
                            score_n = s_pe.n

                            def sc_exp(s, jg=jg, sbank=sbank, score_n=score_n,
                                       nneg=negb_n):
                                s.wait_ge(s_v.h, nneg)
                                s.wait_ge(s_pe.h, score_n)
                                if jg >= 4:
                                    s.wait_ge(s_pe.h, av_cnt[jg - 4])
                                    if (jg - 4) in msk_cnt:
                                        s.wait_ge(s_msk.h, msk_cnt[jg - 4])
                                s_exp.inc(s.activation(e12[jg % 4][:], sbank[:, 0:256],
                                                       AF.Exp, bias=negb_s[:, 0:1],
                                                       scale=INV))
                            block.scalar(sc_exp)
                            exp_cnt[jg] = s_exp.n

                            if j == qb:
                                def v_mask(v, jg=jg, n=s_exp.n):
                                    v.wait_ge(s_exp.h, n)
                                    s_msk.inc(v.tensor_tensor(e12[jg % 4][:], e12[jg % 4][:],
                                                              mask_s[:], ALU.mult))
                                block.vector(v_mask)
                                msk_cnt[jg] = s_msk.n

                            if j > 0:
                                emit_av(j - 1, jg - 1)
                            jg_ctr[0] += 1
                        emit_av(qb, jg_ctr[0] - 1)

                        def v_norm(v, qbg=qbg, par=par, o1=o1, o2=o2, n=s_pe.n):
                            v.wait_ge(s_pe.h, n)
                            if qbg >= 2:
                                v.wait_ge(s_sc.h, sq_cnt[qbg - 2])
                                v.wait_ge(s_gd.h, yst_cnt[qbg - 2])
                            v.reciprocal(tiny2[:, 16:17], o1[:, 256:257])
                            v.reciprocal(tiny2[:, 17:18], o2[:, 256:257])
                            v.drain()
                            v.tensor_tensor(tiny2[:, 18:19], tiny2[:, 17:18],
                                            lamneg_s[:], ALU.mult)
                            v.drain()
                            v.tensor_scalar(odbuf[:], o1[:, 0:256], tiny2[:, 16:17],
                                            None, ALU.mult)
                            v.drain()
                            s_nrm.inc(v.scalar_tensor_tensor(
                                ybuf[par][:], o2[:, 0:256], tiny2[:, 18:19], odbuf[:],
                                ALU.mult, ALU.add, accum_out=tiny2[:, 19:20]))
                        block.vector(v_norm)
                        nrm_n = s_nrm.n
                        nrm_cnt[qbg] = nrm_n

                        def sc_sq(s, par=par, n=nrm_n):
                            s.wait_ge(s_nrm.h, n)
                            s_sc.inc(s.activation(sqout[:], ybuf[par][:], AF.Square,
                                                  accum_out=tiny2[:, 20:21]))
                        block.scalar(sc_sq)
                        sq_cnt[qbg] = s_sc.n

                        def v_stat(v, h=h, n=s_sc.n):
                            v.wait_ge(s_sc.h, n)
                            v.tensor_tensor(stats_s[:, h:h + 1], stats_s[:, h:h + 1],
                                            tiny2[:, 19:20], ALU.add)
                            s_v.inc(v.tensor_tensor(stats_s[:, 16 + h:17 + h],
                                                    stats_s[:, 16 + h:17 + h],
                                                    tiny2[:, 20:21], ALU.add))
                        block.vector(v_stat)

                        def g_y(gp, h=h, qb=qb, par=par, n=nrm_n):
                            gp.wait_ge(s_nrm.h, n)
                            s_gd.inc(gp.dma_start(
                                out=y_d[128 * qb:128 * (qb + 1), 256 * h:256 * (h + 1)],
                                in_=ybuf[par][:]), 16)
                        block.gpsimd(g_y)
                        yst_cnt[qbg] = s_gd.n
                        qbg_ctr[0] += 1

                    pe_h_end[h] = s_pe.n
                pe_g_end[g] = s_pe.n

        if want >= 4:
            p3_pe_end, p3_v_end, p3_gd_end = s_pe.n, s_v.n, s_gd.n
            p3_exp_end, p3_sc_end = s_exp.n, s_sc.n

            def pe_stat(t):
                t.wait_ge(s_v.h, p3_v_end)
                t.wait_ge(s_exp.h, p3_exp_end)
                t.wait_ge(s_sc.h, p3_sc_end)
                s_pe.inc(t.matmul(pA[0:1, 0:32], ones_s[:, 0:1], stats_s[:],
                                  start=True, stop=True))
            block.tensor(pe_stat)

            def sc_stat(s, n=s_pe.n):
                s.wait_ge(s_pe.h, n)
                s_sc.inc(s.copy(tiny[0:1, 0:32], pA[0:1, 0:32]))
            block.scalar(sc_stat)

            def v_stat2(v, n=s_sc.n):
                v.wait_ge(s_sc.h, n)
                v.tensor_scalar(tiny[0:1, 32:48], tiny[0:1, 0:16], 1.0 / GN_N,
                                None, ALU.mult)
                v.tensor_scalar(tiny[0:1, 48:64], tiny[0:1, 16:32], 1.0 / GN_N,
                                None, ALU.mult)
                v.drain()
                v.tensor_tensor(tiny[0:1, 64:80], tiny[0:1, 32:48], tiny[0:1, 32:48],
                                ALU.mult)
                v.drain()
                v.tensor_tensor(tiny[0:1, 80:96], tiny[0:1, 48:64], tiny[0:1, 64:80],
                                ALU.subtract)
                v.drain()
                s_v.inc(v.tensor_scalar(tiny[0:1, 80:96], tiny[0:1, 80:96], EPS,
                                        None, ALU.add))
            block.vector(v_stat2)

            def sc_stat2(s, n=s_v.n):
                s.wait_ge(s_v.h, n)
                s.activation(tiny[0:1, 96:112], tiny[0:1, 80:96], AF.Ln)
                s_sc.inc(s.activation(tiny[0:1, 96:112], tiny[0:1, 96:112], AF.Exp,
                                      scale=-0.5))
            block.scalar(sc_stat2)

            def v_stat3(v, n=s_sc.n):
                v.wait_ge(s_sc.h, n)
                s_v.inc(v.tensor_scalar(tiny[0:1, 48:64], tiny[0:1, 96:112], 0.8,
                                        None, ALU.mult))
            block.vector(v_stat3)

            def pe_bc(t, n=s_v.n):
                t.wait_ge(s_v.h, n)
                s_pe.inc(t.matmul(pB[:, 0:32], ones_row[:], tiny[0:1, 32:64],
                                  start=True, stop=True))
            block.tensor(pe_bc)

            def v_tbl(v, n=s_pe.n):
                v.wait_ge(s_pe.h, n)
                v.tensor_copy(bc32[:], pB[:, 0:32])
                v.drain()
                rstd_v = bc32[:, 16:32].unsqueeze(2).broadcast_to([128, 16, 2])
                mean_v = bc32[:, 0:16].unsqueeze(2).broadcast_to([128, 16, 2])
                gnw_v = gnw_s[:].rearrange("p (h r) -> p h r", h=16)
                gnb_v = gnb_s[:].rearrange("p (h r) -> p h r", h=16)
                sct_v = sctbl_s[:].rearrange("p (h r) -> p h r", h=16)
                tmp_v = tiny[:, 0:32].rearrange("p (h r) -> p h r", h=16)
                bit_v = bitbl_s[:].rearrange("p (h r) -> p h r", h=16)
                v.tensor_tensor(sct_v, gnw_v, rstd_v, ALU.mult)
                v.drain()
                v.tensor_tensor(tmp_v, sct_v, mean_v, ALU.mult)
                v.drain()
                s_v.inc(v.scalar_tensor_tensor(bit_v, gnb_v, 0.8, tmp_v,
                                               ALU.mult, ALU.subtract))
            block.vector(v_tbl)
            tbl_n = s_v.n

            x2 = pool[0:8]
            ystA, ystB, ysel = sqb[0], sqb[1], tstage[0]
            yev_cnt = {}
            woev_cnt = {}
            blend_cnt = {}
            wo_ld = []
            wo_use = []

            for fp in range(8):
                for c in range(8):
                    def g_yld(gp, fp=fp, c=c, nb=blend_cnt.get((fp, c - 1) if c else (fp - 1, 7), 0)):
                        gp.wait_ge(s_v.h, max(nb, tbl_n))
                        if fp == 0 and c == 0:
                            gp.wait_ge(s_gd.h, p3_gd_end)
                            gp.wait_ge(s_pe.h, p3_pe_end)
                        s_gd.inc(gp.dma_start(
                            out=ystA[:], in_=y_d[128 * c:128 * (c + 1),
                                                 512 * fp:512 * (fp + 1)]), 16)
                        s_gd.inc(gp.dma_start(
                            out=ystB[:], in_=y_d[HALF + 128 * c:HALF + 128 * (c + 1),
                                                 512 * fp:512 * (fp + 1)]), 16)
                    block.gpsimd(g_yld)

                    def v_blend(v, n=s_gd.n, fp=fp, c=c):
                        v.wait_ge(s_gd.h, n)
                        if fp == 0 and c < 2:
                            v.wait_ge(s_pe.h, p3_pe_end)
                        v.tensor_scalar(ysel[:], ystA[:], hsel_s[:, 0:1], None, ALU.mult)
                        s_v.inc(v.scalar_tensor_tensor(ysel[:], ystB[:], hsel_s[:, 1:2],
                                                       ysel[:], ALU.mult, ALU.add))
                    block.vector(v_blend)
                    blend_cnt[(fp, c)] = s_v.n

                    bank = (pT0, pT1)[c % 2]

                    def pe_ytp(t, bank=bank, fp=fp, c=c, nb=s_v.n,
                               nprev=None):
                        t.wait_ge(s_v.h, nb)
                        if c >= 2:
                            t.wait_ge(s_v.h, yev_cnt[(fp, c - 2)])
                        elif fp > 0:
                            t.wait_ge(s_v.h, woev_cnt[(fp - 1, 3)])
                        for i in range(4):
                            inst = t.transpose(bank[:, 128 * i:128 * (i + 1)],
                                               ysel[:, 128 * i:128 * (i + 1)], ident_s[:])
                        s_pe.inc(inst)
                    block.tensor(pe_ytp)

                    def v_yev(v, bank=bank, fp=fp, c=c, n=s_pe.n):
                        v.wait_ge(s_pe.h, n)
                        for fk in range(4):
                            fg = 4 * fp + fk
                            inst = v.tensor_scalar(
                                yq[:, 1024 * fk + 128 * c:1024 * fk + 128 * (c + 1)],
                                bank[:, 128 * fk:128 * (fk + 1)],
                                sctbl_s[:, fg:fg + 1], bitbl_s[:, fg:fg + 1],
                                ALU.mult, ALU.add)
                        s_v.inc(inst)
                    block.vector(v_yev)
                    yev_cnt[(fp, c)] = s_v.n
                yall_n = s_v.n

                for dq in range(4):
                    for fk in range(4):
                        def sy_wo(sy, fp=fp, dq=dq, fk=fk):
                            si = len(wo_ld)
                            if si >= 4:
                                sy.wait_ge(s_pe.h, wo_use[si - 4])
                            s_w.inc(sy.dma_start(
                                out=wslice[si % 4][:],
                                in_=wo_d[512 * fp + 128 * fk:512 * fp + 128 * (fk + 1),
                                         512 * dq:512 * (dq + 1)]), 16)
                            wo_ld.append(s_w.n)
                        block.sync(sy_wo)

                        def pe_wo(t, fp=fp, dq=dq, fk=fk, si=len(wo_ld) - 1,
                                  yn=yall_n):
                            t.wait_ge(s_w.h, wo_ld[si])
                            t.wait_ge(s_v.h, yn)
                            if fk == 0 and dq > 0:
                                t.wait_ge(s_v.h, woev_cnt[(fp, dq - 1)])
                            banks = [pU[0], pU[1], pU[2], pU[3], pA, pB, pT0, pT1]
                            for c in range(8):
                                inst = t.matmul(banks[c][:],
                                                yq[:, 1024 * fk + 128 * c:
                                                   1024 * fk + 128 * (c + 1)],
                                                wslice[si % 4][:],
                                                start=(fk == 0), stop=(fk == 3))
                            s_pe.inc(inst)
                            wo_use.append(s_pe.n)
                        block.tensor(pe_wo)

                    def v_woev(v, fp=fp, dq=dq, n=s_pe.n):
                        v.wait_ge(s_pe.h, n)
                        banks = [pU[0], pU[1], pU[2], pU[3], pA, pB, pT0, pT1]
                        for c in range(8):
                            if fp == 0:
                                inst = v.tensor_copy(x2[c][:, 512 * dq:512 * (dq + 1)],
                                                     banks[c][:])
                            else:
                                inst = v.tensor_tensor(x2[c][:, 512 * dq:512 * (dq + 1)],
                                                       x2[c][:, 512 * dq:512 * (dq + 1)],
                                                       banks[c][:], ALU.add)
                        s_v.inc(inst)
                    block.vector(v_woev)
                    woev_cnt[(fp, dq)] = s_v.n

            if dbg and want == 4:
                for c in range(8):
                    def g_x2dbg(gp, c=c, n=s_v.n):
                        gp.wait_ge(s_v.h, n)
                        s_out.inc(gp.dma_start(out=x2_dbg[128 * c:128 * (c + 1), :],
                                               in_=x2[c][:]), 16)
                    block.gpsimd(g_x2dbg)

        if want >= 5:
            p4_v_end, p4_pe_end = s_v.n, s_pe.n
            x1A, x1B, x1sel, xa3_tm = pool[8], pool[9], pool[10], pool[11]
            out_prev = [0]

            for c in range(8):
                p = c % 2

                def g_x1ld(gp, c=c, nout=out_prev[0]):
                    gp.wait_ge(s_out.h, nout)
                    gp.wait_ge(s_v.h, p4_v_end)
                    s_gd.inc(gp.dma_start(out=x1A[:],
                                          in_=x1_d[128 * c:128 * (c + 1), :]), 16)
                    s_gd.inc(gp.dma_start(out=x1B[:],
                                          in_=x1_d[HALF + 128 * c:HALF + 128 * (c + 1), :]), 16)
                block.gpsimd(g_x1ld)

                def v_x1b(v, c=c, n=s_gd.n):
                    v.wait_ge(s_gd.h, n)
                    v.tensor_scalar(x1sel[:], x1A[:], hsel_s[:, 0:1], None, ALU.mult)
                    v.scalar_tensor_tensor(x1sel[:], x1B[:], hsel_s[:, 1:2],
                                           x1sel[:], ALU.mult, ALU.add)
                    v.tensor_tensor(x2[c][:], x2[c][:], x1sel[:], ALU.add)
                    s_v.inc(ln_sum_v(v, x2[c][:], sm_b[p]))
                block.vector(v_x1b)

                def sc_sq5(s, c=c, p=p, n=s_v.n):
                    s.wait_ge(s_v.h, n)
                    s_sc.inc(ln_sq_sc(s, x2[c][:], x1sel[:], sm_b[p]))
                block.scalar(sc_sq5)

                def v_fin5(v, p=p, n=s_sc.n):
                    v.wait_ge(s_sc.h, n)
                    s_v.inc(ln_fin_v(v, sm_b[p]))
                block.vector(v_fin5)

                def sc_ln5(s, p=p, n=s_v.n):
                    s.wait_ge(s_v.h, n)
                    s_sc.inc(rstd_sc(s, sm_b[p]))
                block.scalar(sc_ln5)

                def v_xa5(v, c=c, p=p, n=s_sc.n):
                    v.wait_ge(s_sc.h, n)
                    s_v.inc(v.tensor_scalar(xa3_tm[:], x2[c][:], sm_b[p][:, 24:25],
                                            sm_b[p][:, 26:27], ALU.subtract, ALU.mult))
                block.vector(v_xa5)

                def pe_pre5(t, n=s_v.n, np4=p4_pe_end):
                    t.wait_ge(s_v.h, n)
                transposes16(xa3_tm, x1A, pre_wait=pe_pre5)

                emit_ffn(p, x1A[:], x2[c][:], x1B[:],
                         [(s_evac.h, s_evac.n), (s_gd.h, s_gd.n)])

                def sy_out(sy, c=c, n=s_v.n):
                    sy.wait_ge(s_v.h, n)
                    s_out.inc(sy.dma_start(out=out_d[128 * c:128 * (c + 1), :],
                                           in_=x1B[:]), 16)
                block.sync(sy_out)
                out_prev[0] = s_out.n

        def fin_g(g):
            g.wait_ge(s_gd.h, s_gd.n)
        block.gpsimd(fin_g)

        def fin_sy(sy):
            sy.wait_ge(s_gd.h, s_gd.n)
            sy.wait_ge(s_out.h, s_out.n)
            if dbg:
                pass
        block.sync(fin_sy)

        if dbg:
            def dbg_dump(sy):
                s_v.wait(sy)
                s_sc.wait(sy)
                s_out.inc(sy.dma_start(out=dbg_d, in_=tiny[:]), 16)
                sy.wait_ge(s_out.h, s_out.n)
            block.sync(dbg_dump)

    return nc



_NC_CACHE = {}


def _get_nc(stage):
    if stage not in _NC_CACHE:
        _NC_CACHE[stage] = build_nc(stage)
    return _NC_CACHE[stage]


def make_in_maps(inputs, stage="full"):
    ident, mask2, cos_t, sin_t, e8 = _host_constants()
    w = _prep_weights(inputs)
    x = np.ascontiguousarray(np.asarray(inputs["L0_attn_x"], dtype=np.float32))
    in_maps = []
    for c in range(8):
        b, half = c // 2, c % 2
        hs = np.zeros((128, 2), dtype=np.float32)
        hs[:, 0] = 1.0 - half
        hs[:, 1] = float(half)
        m = {"x": x[b], "ident": ident, "mask2": mask2, "cos_t": cos_t,
             "sin_t": sin_t, "e8": e8, "halfsel": hs}
        m.update(w)
        in_maps.append(m)
    return in_maps


def kernel(**inputs):
    nc = _get_nc("full")
    in_maps = make_in_maps(inputs)
    res = run_bass_kernel_spmd(nc, in_maps, core_ids=list(range(8)))
    out = np.empty((B, T, D), dtype=np.float32)
    for c in range(8):
        b, half = c // 2, c % 2
        out[b, half * HALF:(half + 1) * HALF, :] = res.results[c]["out"]
    return out
